# version 05
# baseline (speedup 1.0000x reference)
import math
import numpy as np
import concourse.bass as bass
import concourse.mybir as mybir
from concourse.bass_utils import run_bass_kernel_spmd

F32 = mybir.dt.float32
BF16 = mybir.dt.bfloat16
I32 = mybir.dt.int32
AF = mybir.ActivationFunctionType
ALU = mybir.AluOpType
AX = mybir.AxisListType

D = 4096
INT = 8192
LAT = 2048
NE = 8
SEQ = 8192
NCORES = 8
TPC = SEQ // NCORES
TILE = 256
NT = TPC // TILE
NS = 512
EPS = 1e-5
RING = 8
KC_D = D // 128
KC_I = INT // 128
KC_L = LAT // 128


def _relayout(w):
    K, N = w.shape
    return np.ascontiguousarray(
        w.reshape(K // 128, 128, N // NS, NS).transpose(2, 0, 1, 3).reshape(-1, NS))


class Sem:
    def __init__(self, h):
        self.h = h
        self.n = 0

    def inc(self, instr, k=1):
        instr.then_inc(self.h, k)
        self.n += k
        return instr


_PH = ["ln1", "gate", "up", "down", "fc1", "exp", "fc2", "gbc", "ln2",
       "attn", "comb"]


def build_nc(upto="comb", ntiles=NT):
    lim = _PH.index(upto)

    def stage(name):
        return _PH.index(name) <= lim

    nc = bass.Bass("TRN2", target_bir_lowering=False, debug=False)

    x_h = nc.dram_tensor("x", [TPC, D], F32, kind="ExternalInput").ap()
    out_h = nc.dram_tensor("out", [TPC, D], F32, kind="ExternalOutput").ap()
    gate_h = nc.dram_tensor("gate", [128, KC_D * NE], F32, kind="ExternalInput").ap()
    gb_h = nc.dram_tensor("gb", [1, 2 * D], F32, kind="ExternalInput").ap()
    wa_h = nc.dram_tensor("wa", [(D // NS) * KC_D * 128, NS], F32, kind="ExternalInput").ap()
    ma_h = nc.dram_tensor("ma", [(D // NS) * KC_D * 128, NS], I32, kind="ExternalInput").ap()
    wu_h = nc.dram_tensor("wu", [(INT // NS) * KC_D * 128, NS], F32, kind="ExternalInput").ap()
    wd_h = nc.dram_tensor("wd", [(D // NS) * KC_I * 128, NS], F32, kind="ExternalInput").ap()
    w1_h = nc.dram_tensor("w1", [(LAT // NS) * KC_D * 128, NS], F32, kind="ExternalInput").ap()
    we_h = nc.dram_tensor("we", [(LAT // NS) * KC_L * 128, NS], F32, kind="ExternalInput").ap()
    w2_h = nc.dram_tensor("w2", [(D // NS) * KC_L * 128, NS], F32, kind="ExternalInput").ap()

    S = lambda n_, s, dt=F32: nc.alloc_sbuf_tensor(n_, s, dt).ap()
    xn = S("xn", [128, KC_D * 256], BF16)
    xnl = S("xnl", [128, KC_D * 256], BF16)
    hb = S("hb", [128, KC_I * 256])
    u32 = S("u32", [128, 2 * KC_L * 256])
    sr = S("sr", [128, 2 * D])
    stg = S("stg", [128, 4 * NS])
    wring = S("wring", [128, RING * NS])
    whi = S("whi", [128, RING * NS], BF16)
    wlo = S("wlo", [128, RING * NS], BF16)
    gate_s = S("gate_s", [128, KC_D * NE])
    gate_hi = S("gate_hi", [128, KC_D * NE], BF16)
    gate_lo = S("gate_lo", [128, KC_D * NE], BF16)
    ones1 = S("ones1", [1, 128])
    ident = S("ident", [128, 128])
    iot = S("iot", [128, 128], I32)
    logit = S("logit", [128, 2 * NE])
    lm = S("lm", [128, 2 * NE])
    sm = {k: S(k, [128, 2]) for k in
          ["route", "m1", "cnt", "m2", "tta", "ttb", "red", "ssq",
           "mu", "msq", "mu2", "veps", "stdv", "rstd"]}

    P = lambda n_: nc.alloc_psum_tensor(n_, [128, NS], F32).ap()
    psA = [P("psA0"), P("psA1")]
    psB = [P("psB0"), P("psB1")]
    pT = [P("pT0"), P("pT1")]
    psG = P("psG")
    psG2 = P("psG2")

    xld = [hb[:, 0:D], hb[:, D:2 * D]]
    xnt = [hb[:, 2 * D:3 * D], hb[:, 3 * D:4 * D]]
    gbrow = hb[0:1, 2 * D:4 * D]
    hbb = hb.bitcast(BF16)
    u32b = u32.bitcast(BF16)
    HL = KC_I * 256
    UL = KC_L * 256
    ck = lambda t, off, c, tc: t[:, off + c * 256 + tc * 128: off + c * 256 + tc * 128 + 128]
    latc = lambda lc, tc: ck(u32b, 0, lc, tc)
    latlc = lambda lc, tc: ck(u32b, UL, lc, tc)
    moec = lambda lc, tc: ck(u32b, 2 * UL, lc, tc)
    moelc = lambda lc, tc: ck(u32b, 3 * UL, lc, tc)
    gbc = u32[:, 0:D]
    bbc = u32[:, D:2 * D]
    xnc = lambda kc, tc: ck(xn, 0, kc, tc)
    xnlc = lambda kc, tc: ck(xnl, 0, kc, tc)
    hc = lambda ic, tc: ck(hbb, 0, ic, tc)
    hlc = lambda ic, tc: ck(hbb, HL, ic, tc)
    srt = lambda tc: sr[:, tc * D:(tc + 1) * D]

    with (
        nc.semaphore("wsA") as _wsA,
        nc.semaphore("wsB") as _wsB,
        nc.semaphore("msem") as _ms,
        nc.semaphore("xsem") as _xs,
        nc.semaphore("aux") as _ax,
        nc.semaphore("ucons") as _uc,
        nc.semaphore("tmm") as _tmm,
        nc.semaphore("ev") as _ev,
        nc.semaphore("dv") as _dv,
        nc.Block() as block,
    ):
        wsem = [Sem(_wsA), Sem(_wsB)]
        msem, xsem, aux = Sem(_ms), Sem(_xs), Sem(_ax)
        ucons, tmm, ev, dv = Sem(_uc), Sem(_tmm), Sem(_ev), Sem(_dv)
        last_wait = {}

        def wait(eng, ename, sem, val):
            if val <= 0:
                return
            key = (ename, sem.h.name if hasattr(sem.h, "name") else id(sem.h))
            if last_wait.get(key, -1) >= val:
                return
            last_wait[key] = val
            eng.wait_ge(sem.h, val)

        TEN, VEC, ACT, SYN, GPS = nc.tensor, nc.vector, nc.scalar, nc.sync, nc.gpsimd

        gU = [0]
        bank_free = {}
        stg_free = {}
        pT_free = {}
        state = {"sr_free": [(dv, 0), (dv, 0)], "u32_free": (dv, 0),
                 "logit_free": (dv, 0), "stg_mask_free": (dv, 0),
                 "route_free": (dv, 0), "mask_dv": 0}

        def wslot_ap(u):
            s = u % RING
            return wring[:, s * NS:(s + 1) * NS]

        def issue_units(w_hbm, base_unit, count):
            assert count == 4
            u0 = gU[0]
            b = u0 // 4
            need = math.ceil((u0 + count - RING) / 4)
            wait(SYN, "syn", ucons, need)
            half = wsem[b % 2]
            for j in range(count):
                r = (base_unit + j) * 128
                d = SYN.dma_start(out=wslot_ap(u0 + j), in_=w_hbm[r:r + 128, :])
                half.inc(d, 16)
            gU[0] += count
            return half, half.n

        def mask_issue4(m_hbm, base_unit):
            for j in range(4):
                r = (base_unit + j) * 128
                d = GPS.dma_start(out=stg[:, j * NS:(j + 1) * NS], in_=m_hbm[r:r + 128, :])
                msem.inc(d, 16)
            return msem.n

        def accum_phase(tag, nslabs, kcs, par0, lhsT_hi, lhsT_lo, w_hbm,
                        w_base_unit, evac_fn, m_hbm=None):
            for s_ in range(nslabs):
                par = (par0 + s_) % 2
                banks = psA if par == 0 else psB
                bname = ("psA", "psB")[par]
                for b, pb in zip(banks, (bname + "0", bname + "1")):
                    if pb in bank_free:
                        sem, val = bank_free[pb]
                        wait(TEN, "ten", sem, val)
                for kb in range(kcs // 4):
                    base = s_ * kcs + kb * 4
                    bglob = gU[0] // 4
                    wsm, arr = issue_units(w_hbm, w_base_unit + base, 4)
                    u0 = gU[0] - 4
                    s0 = u0 % RING
                    span = slice(s0 * NS, (s0 + 4) * NS)
                    hname = "AB"[wsm is wsem[1]]
                    if m_hbm is not None:
                        wait(GPS, "gps", dv, state["mask_dv"])
                        marr = mask_issue4(m_hbm, w_base_unit + base)
                        wait(VEC, "vec" + hname, wsm, arr)
                        wait(VEC, "vec", msem, marr)
                        t = VEC.tensor_tensor(wring[:, span], wring[:, span],
                                              stg[:, 0:4 * NS], ALU.mult)
                        dv.inc(t)
                        state["mask_dv"] = dv.n
                        wait(ACT, "act", dv, dv.n)
                    else:
                        wait(ACT, "act" + hname, wsm, arr)
                    wait(ACT, "act", ucons, bglob - 1)
                    a_ = ACT.copy(whi[:, span], wring[:, span])
                    ev.inc(a_)
                    wait(VEC, "vec", ev, ev.n)
                    t = VEC.tensor_tensor(wlo[:, span], wring[:, span],
                                          whi[:, span], ALU.subtract)
                    dv.inc(t)
                    wait(TEN, "ten", dv, dv.n)
                    for j in range(4):
                        kc = kb * 4 + j
                        u = u0 + j
                        sl_hi = whi[:, (u % RING) * NS:(u % RING + 1) * NS]
                        sl_lo = wlo[:, (u % RING) * NS:(u % RING + 1) * NS]
                        st_ = kc == 0
                        sp_ = kc == kcs - 1
                        for tc in range(2):
                            TEN.matmul(banks[tc], lhsT_hi(kc, tc), sl_hi,
                                       start=st_, stop=False)
                            TEN.matmul(banks[tc], lhsT_hi(kc, tc), sl_lo,
                                       start=False, stop=False)
                            m1_ = TEN.matmul(banks[tc], lhsT_lo(kc, tc), sl_hi,
                                             start=False, stop=sp_)
                        if j == 3:
                            ucons.inc(m1_)
                evac_fn(s_, banks, (bname + "0", bname + "1"), ucons.n)

        def stage_to_fm(tc, src_slab, col_hi, col_lo, dep_sem, dep_val):
            if tc in pT_free:
                sem, val = pT_free[tc]
                wait(TEN, "ten", sem, val)
            wait(TEN, "ten", dep_sem, dep_val)
            for j in range(4):
                t = TEN.transpose(pT[tc][:, j * 128:(j + 1) * 128],
                                  src_slab[:, j * 128:(j + 1) * 128], ident)
            tmm.inc(t)
            wait(ACT, "act", tmm, tmm.n)
            for j in range(4):
                c = ACT.copy(col_hi(j), pT[tc][:, j * 128:(j + 1) * 128])
            ev.inc(c)
            wait(VEC, "vec", ev, ev.n)
            for j in range(4):
                v = VEC.tensor_tensor(col_lo(j), pT[tc][:, j * 128:(j + 1) * 128],
                                      col_hi(j), ALU.subtract)
            dv.inc(v)
            pT_free[tc] = (dv, dv.n)

        GPS.iota(iot, [[1, 128]], base=0, channel_multiplier=-1)
        d = GPS.dma_start(out=gate_s, in_=gate_h)
        aux.inc(d, 16)
        aux_setup = aux.n
        VEC.memset(ones1, 1.0)
        wait(VEC, "vec", aux, aux_setup)
        v = VEC.tensor_scalar(ident, iot, 0.0, None, ALU.is_equal)
        dv.inc(v)
        wait(ACT, "act", aux, aux_setup)
        a = ACT.copy(gate_hi, gate_s)
        ev.inc(a)
        wait(VEC, "vec", ev, ev.n)
        v = VEC.tensor_tensor(gate_lo, gate_s, gate_hi, ALU.subtract)
        dv.inc(v)
        setup_dv = dv.n

        for t in range(ntiles):
            r0 = t * TILE

            wait(GPS, "gps", ucons, ucons.n)
            wait(GPS, "gps", tmm, tmm.n)
            wait(GPS, "gps", ev, ev.n)
            wait(GPS, "gps", xsem, xsem.n)
            for tc in range(2):
                d = GPS.dma_start(out=xld[tc], in_=x_h[r0 + tc * 128: r0 + (tc + 1) * 128, :])
                xsem.inc(d, 16)
            xarr = xsem.n
            wait(VEC, "vec", xsem, xarr)
            wait(ACT, "act", xsem, xarr)
            for tc in range(2):
                v = VEC.reduce_sum(sm["red"][:, tc:tc + 1], xld[tc], axis=AX.X)
            dv.inc(v)
            wait(VEC, "vec", dv, dv.n)
            for tc in range(2):
                a = ACT.activation(xnt[tc], xld[tc], AF.Square,
                                   accum_out=sm["ssq"][:, tc:tc + 1])
            ev.inc(a)
            for tc in range(2):
                v = VEC.tensor_scalar(sm["mu"][:, tc:tc + 1], sm["red"][:, tc:tc + 1],
                                      1.0 / D, None, ALU.mult)
            dv.inc(v)
            wait(VEC, "vec", dv, dv.n)
            wait(VEC, "vec", ev, ev.n)
            for tc in range(2):
                VEC.tensor_scalar(sm["msq"][:, tc:tc + 1], sm["ssq"][:, tc:tc + 1],
                                  1.0 / D, EPS, ALU.mult, ALU.add)
            for tc in range(2):
                v = VEC.tensor_tensor(sm["mu2"][:, tc:tc + 1], sm["mu"][:, tc:tc + 1],
                                      sm["mu"][:, tc:tc + 1], ALU.mult)
            dv.inc(v)
            wait(VEC, "vec", dv, dv.n)
            for tc in range(2):
                v = VEC.tensor_tensor(sm["veps"][:, tc:tc + 1], sm["msq"][:, tc:tc + 1],
                                      sm["mu2"][:, tc:tc + 1], ALU.subtract)
            dv.inc(v)
            wait(ACT, "act", dv, dv.n)
            for tc in range(2):
                a = ACT.activation(sm["stdv"][:, tc:tc + 1], sm["veps"][:, tc:tc + 1], AF.Sqrt)
            ev.inc(a)
            wait(VEC, "vec", ev, ev.n)
            for tc in range(2):
                v = VEC.reciprocal(sm["rstd"][:, tc:tc + 1], sm["stdv"][:, tc:tc + 1])
            dv.inc(v)
            wait(VEC, "vec", dv, dv.n)
            apply_dv = []
            for tc in range(2):
                v = VEC.tensor_scalar(xnt[tc], xld[tc], sm["mu"][:, tc:tc + 1],
                                      sm["rstd"][:, tc:tc + 1], ALU.subtract, ALU.mult)
                dv.inc(v)
                apply_dv.append(dv.n)
            for tc in range(2):
                for q in range(8):
                    stage_to_fm(tc, xnt[tc][:, q * NS:(q + 1) * NS],
                                lambda j, q=q, tc=tc: xnc(q * 4 + j, tc),
                                lambda j, q=q, tc=tc: xnlc(q * 4 + j, tc),
                                dv, apply_dv[tc])
            xn_ev = ev.n
            xn_dv = dv.n

            if not stage("gate"):
                continue
            if "psG" in bank_free:
                sem, val = bank_free["psG"]
                wait(TEN, "ten", sem, val)
            wait(TEN, "ten", ev, xn_ev)
            wait(TEN, "ten", aux, aux_setup)
            if "psG2" in bank_free:
                sem, val = bank_free["psG2"]
                wait(TEN, "ten", sem, val)
            wait(TEN, "ten", dv, max(setup_dv, xn_dv))
            for kc in range(KC_D):
                st_ = kc == 0
                sp_ = kc == KC_D - 1
                ghi = gate_hi[:, kc * NE:(kc + 1) * NE]
                glo = gate_lo[:, kc * NE:(kc + 1) * NE]
                for tc, pg in ((0, psG), (1, psG2)):
                    TEN.matmul(pg[:, 0:NE], xnc(kc, tc), ghi, start=st_, stop=False)
                    TEN.matmul(pg[:, 0:NE], xnc(kc, tc), glo, start=False, stop=False)
                    m_ = TEN.matmul(pg[:, 0:NE], xnlc(kc, tc), ghi, start=False, stop=sp_)
            tmm.inc(m_)
            sem, val = state["logit_free"]
            wait(ACT, "act", sem, val)
            wait(ACT, "act", tmm, tmm.n)
            a = ACT.copy(logit[:, 0:NE], psG[:, 0:NE])
            a = ACT.copy(logit[:, NE:2 * NE], psG2[:, 0:NE])
            ev.inc(a)
            bank_free["psG"] = (ev, ev.n)
            bank_free["psG2"] = (ev, ev.n)
            wait(VEC, "vec", ev, ev.n)
            sem, val = state["route_free"]
            wait(VEC, "vec", sem, val)
            l8 = lambda tc: logit[:, tc * NE:(tc + 1) * NE]
            c1 = lambda nm, tc: sm[nm][:, tc:tc + 1]

            def vfence(instr):
                dv.inc(instr)
                wait(VEC, "vec", dv, dv.n)

            for tc in range(2):
                v = VEC.reduce_max(c1("m1", tc), l8(tc), axis=AX.X)
            vfence(v)
            for tc in range(2):
                v = VEC.tensor_scalar(lm[:, tc * NE:(tc + 1) * NE], l8(tc),
                                      c1("m1", tc), None, ALU.is_ge)
            vfence(v)
            for tc in range(2):
                v = VEC.reduce_sum(c1("cnt", tc), lm[:, tc * NE:(tc + 1) * NE], axis=AX.X)
            vfence(v)
            for tc in range(2):
                v = VEC.scalar_tensor_tensor(lm[:, tc * NE:(tc + 1) * NE],
                                             lm[:, tc * NE:(tc + 1) * NE], -1e30,
                                             l8(tc), ALU.mult, ALU.add)
            vfence(v)
            for tc in range(2):
                v = VEC.reduce_max(c1("m2", tc), lm[:, tc * NE:(tc + 1) * NE], axis=AX.X)
            vfence(v)
            for tc in range(2):
                v = VEC.tensor_scalar(c1("cnt", tc), c1("cnt", tc), 1.5, None, ALU.is_ge)
            vfence(v)
            for tc in range(2):
                v = VEC.tensor_tensor(c1("tta", tc), c1("m1", tc), c1("m2", tc), ALU.subtract)
            for tc in range(2):
                v2 = VEC.tensor_tensor(c1("ttb", tc), c1("m1", tc), c1("m2", tc), ALU.add)
            vfence(v2)
            for tc in range(2):
                v = VEC.tensor_tensor(c1("tta", tc), c1("tta", tc), c1("cnt", tc), ALU.mult)
            vfence(v)
            for tc in range(2):
                v = VEC.tensor_tensor(c1("route", tc), c1("ttb", tc), c1("tta", tc), ALU.add)
            dv.inc(v)
            state["logit_free"] = (dv, dv.n)
            route_dv = dv.n
            if upto == "gate":
                wait(GPS, "gps", dv, dv.n)
                for nm_i, nm in enumerate(["m1", "m2", "cnt", "route"]):
                    d = GPS.dma_start(out=out_h[nm_i * 128:(nm_i + 1) * 128, 0:2],
                                      in_=sm[nm])
                    xsem.inc(d, 16)
                d = GPS.dma_start(out=out_h[512:640, 0:2 * NE], in_=logit)
                xsem.inc(d, 16)

            if not stage("up"):
                continue
            sem, val = state["stg_mask_free"]

            def evac_up(s_, banks, bn, tval):
                for tc in range(2):
                    slab = stg[:, ((s_ % 2) * 2 + tc) * NS:((s_ % 2) * 2 + tc + 1) * NS]
                    key = (s_ % 2) * 2 + tc
                    if key in stg_free:
                        sm_, vl_ = stg_free[key]
                        wait(ACT, "act", sm_, vl_)
                    wait(ACT, "act", ucons, tval)
                    a = ACT.activation(slab, banks[tc], AF.Relu)
                    ev.inc(a)
                    bank_free[bn[tc]] = (ev, ev.n)
                    wait(VEC, "vec", ev, ev.n)
                    v = VEC.tensor_tensor(slab, slab, slab, ALU.mult)
                    dv.inc(v)
                    stage_to_fm(tc, slab,
                                lambda j, s_=s_, tc=tc: hc(s_ * 4 + j, tc),
                                lambda j, s_=s_, tc=tc: hlc(s_ * 4 + j, tc),
                                dv, dv.n)
                    stg_free[key] = (tmm, tmm.n)

            wait(ACT, "act", sem, val)
            wait(TEN, "ten", ev, xn_ev)
            wait(TEN, "ten", dv, xn_dv)
            accum_phase("up", INT // NS, KC_D, 0, xnc, xnlc, wu_h, 0, evac_up)
            h_ev = ev.n
            h_dv = dv.n

            if not stage("down"):
                continue
            def evac_down(s_, banks, bn, tval):
                wait(ACT, "act", ucons, tval)
                for tc in range(2):
                    sm_, vl_ = state["sr_free"][tc]
                    wait(ACT, "act", sm_, vl_)
                    a = ACT.copy(srt(tc)[:, s_ * NS:(s_ + 1) * NS], banks[tc])
                    ev.inc(a)
                    bank_free[bn[tc]] = (ev, ev.n)

            wait(TEN, "ten", ev, h_ev)
            wait(TEN, "ten", dv, h_dv)
            accum_phase("down", D // NS, KC_I, 0, hc, hlc, wd_h, 0, evac_down)

            if not stage("fc1"):
                continue
            sem, val = state["u32_free"]

            def evac_fc1(s_, banks, bn, tval):
                for tc in range(2):
                    slab = stg[:, ((s_ % 2) * 2 + tc) * NS:((s_ % 2) * 2 + tc + 1) * NS]
                    key = (s_ % 2) * 2 + tc
                    if key in stg_free:
                        sm_, vl_ = stg_free[key]
                        wait(ACT, "act", sm_, vl_)
                    wait(ACT, "act", ucons, tval)
                    sm2_, vl2_ = state["u32_free"]
                    wait(ACT, "act", sm2_, vl2_)
                    a = ACT.copy(slab, banks[tc])
                    ev.inc(a)
                    bank_free[bn[tc]] = (ev, ev.n)
                    stage_to_fm(tc, slab,
                                lambda j, s_=s_, tc=tc: latc(s_ * 4 + j, tc),
                                lambda j, s_=s_, tc=tc: latlc(s_ * 4 + j, tc),
                                ev, ev.n)
                    stg_free[key] = (tmm, tmm.n)

            accum_phase("fc1", LAT // NS, KC_D, 0, xnc, xnlc, w1_h, 0, evac_fc1)
            lat_ev = ev.n
            lat_dv = dv.n

            if not stage("exp"):
                continue
            def evac_exp(s_, banks, bn, tval):
                for tc in range(2):
                    slab = stg[:, ((s_ % 2) * 2 + tc) * NS:((s_ % 2) * 2 + tc + 1) * NS]
                    key = (s_ % 2) * 2 + tc
                    if key in stg_free:
                        sm_, vl_ = stg_free[key]
                        wait(VEC, "vec", sm_, vl_)
                    wait(VEC, "vec", ucons, tval)
                    v = VEC.tensor_scalar(slab, banks[tc], sm["route"][:, tc:tc + 1],
                                          None, ALU.mult)
                    dv.inc(v)
                    bank_free[bn[tc]] = (dv, dv.n)
                    stage_to_fm(tc, slab,
                                lambda j, s_=s_, tc=tc: moec(s_ * 4 + j, tc),
                                lambda j, s_=s_, tc=tc: moelc(s_ * 4 + j, tc),
                                dv, dv.n)
                    stg_free[key] = (tmm, tmm.n)

            wait(TEN, "ten", ev, lat_ev)
            wait(TEN, "ten", dv, lat_dv)
            accum_phase("exp", LAT // NS, KC_L, 0, latc, latlc, we_h, 0, evac_exp)
            moe_ev = ev.n
            moe_dv = dv.n

            if not stage("fc2"):
                continue
            def evac_fc2(s_, banks, bn, tval):
                wait(VEC, "vec", ucons, tval)
                for tc in range(2):
                    dst = srt(tc)[:, s_ * NS:(s_ + 1) * NS]
                    v = VEC.tensor_tensor(dst, dst, banks[tc], ALU.add)
                    dv.inc(v)
                    bank_free[bn[tc]] = (dv, dv.n)

            wait(TEN, "ten", ev, moe_ev)
            wait(TEN, "ten", dv, moe_dv)
            accum_phase("fc2", D // NS, KC_L, 0, moec, moelc, w2_h, 0, evac_fc2)
            fc2_dv = dv.n

            if not stage("gbc"):
                continue
            wait(GPS, "gps", tmm, tmm.n)
            wait(GPS, "gps", ucons, ucons.n)
            d = GPS.dma_start(out=gbrow, in_=gb_h)
            aux.inc(d, 16)
            wait(TEN, "ten", aux, aux.n)
            pingpong = [psG, pT[0], pT[1]]
            ppn = ["psG", "pTg0", "pTg1"]
            for i in range(2 * D // NS):
                pp = pingpong[i % 3]
                nm = ppn[i % 3]
                if nm in bank_free:
                    sem, val = bank_free[nm]
                    wait(TEN, "ten", sem, val)
                if nm.startswith("pTg") and (i % 3) - 1 in pT_free:
                    sem, val = pT_free[(i % 3) - 1]
                    wait(TEN, "ten", sem, val)
                m_ = TEN.matmul(pp, ones1, gbrow[0:1, i * NS:(i + 1) * NS],
                                start=True, stop=True)
                tmm.inc(m_)
                wait(ACT, "act", tmm, tmm.n)
                a = ACT.copy(u32[:, i * NS:(i + 1) * NS], pp)
                ev.inc(a)
                bank_free[nm] = (ev, ev.n)
                if nm.startswith("pTg"):
                    pT_free[(i % 3) - 1] = (ev, ev.n)
            gbc_ev = ev.n

            if not stage("ln2"):
                continue
            wait(VEC, "vec", dv, fc2_dv)
            for tc in range(2):
                v = VEC.reduce_sum(sm["red"][:, tc:tc + 1], srt(tc), axis=AX.X)
            dv.inc(v)
            wait(VEC, "vec", dv, dv.n)
            wait(ACT, "act", dv, fc2_dv)
            for tc in range(2):
                a = ACT.activation(xld[tc], srt(tc), AF.Square,
                                   accum_out=sm["ssq"][:, tc:tc + 1])
            ev.inc(a)
            for tc in range(2):
                v = VEC.tensor_scalar(sm["mu"][:, tc:tc + 1], sm["red"][:, tc:tc + 1],
                                      1.0 / D, None, ALU.mult)
            dv.inc(v)
            wait(VEC, "vec", dv, dv.n)
            wait(VEC, "vec", ev, ev.n)
            for tc in range(2):
                VEC.tensor_scalar(sm["msq"][:, tc:tc + 1], sm["ssq"][:, tc:tc + 1],
                                  1.0 / D, EPS, ALU.mult, ALU.add)
            for tc in range(2):
                v = VEC.tensor_tensor(sm["mu2"][:, tc:tc + 1], sm["mu"][:, tc:tc + 1],
                                      sm["mu"][:, tc:tc + 1], ALU.mult)
            dv.inc(v)
            wait(VEC, "vec", dv, dv.n)
            for tc in range(2):
                v = VEC.tensor_tensor(sm["veps"][:, tc:tc + 1], sm["msq"][:, tc:tc + 1],
                                      sm["mu2"][:, tc:tc + 1], ALU.subtract)
            dv.inc(v)
            wait(ACT, "act", dv, dv.n)
            for tc in range(2):
                a = ACT.activation(sm["stdv"][:, tc:tc + 1], sm["veps"][:, tc:tc + 1], AF.Sqrt)
            ev.inc(a)
            ln2sq_ev = ev.n
            wait(VEC, "vec", ev, ev.n)
            for tc in range(2):
                v = VEC.reciprocal(sm["rstd"][:, tc:tc + 1], sm["stdv"][:, tc:tc + 1])
            dv.inc(v)
            wait(VEC, "vec", dv, dv.n)
            wait(VEC, "vec", ev, gbc_ev)
            for tc in range(2):
                v = VEC.tensor_scalar(srt(tc), srt(tc), sm["mu"][:, tc:tc + 1],
                                      sm["rstd"][:, tc:tc + 1], ALU.subtract, ALU.mult)
            dv.inc(v)
            wait(VEC, "vec", dv, dv.n)
            for tc in range(2):
                v = VEC.tensor_tensor(srt(tc), srt(tc), gbc, ALU.mult)
            dv.inc(v)
            wait(VEC, "vec", dv, dv.n)
            for tc in range(2):
                v = VEC.tensor_tensor(srt(tc), srt(tc), bbc, ALU.add)
            dv.inc(v)
            state["u32_free"] = (dv, dv.n)
            state["route_free"] = (dv, dv.n)

            if not stage("attn"):
                continue
            def evac_attn(s_, banks, bn, tval):
                wait(VEC, "vec", ucons, tval)
                for tc in range(2):
                    dst = srt(tc)[:, s_ * NS:(s_ + 1) * NS]
                    v = VEC.tensor_tensor(dst, dst, banks[tc], ALU.add)
                    dv.inc(v)
                    bank_free[bn[tc]] = (dv, dv.n)

            for key in range(4):
                if key in stg_free:
                    sm_, vl_ = stg_free[key]
                    wait(GPS, "gps", sm_, vl_)
            accum_phase("attn", D // NS, KC_D, 0, xnc, xnlc, wa_h, 0, evac_attn,
                        m_hbm=ma_h)
            state["stg_mask_free"] = (dv, dv.n)
            attn_dv = dv.n
            state["sr_free"] = [(dv, attn_dv), (dv, attn_dv)]

            if not stage("comb"):
                continue
            wait(GPS, "gps", ev, ln2sq_ev)
            for tc in range(2):
                d = GPS.dma_start(out=xld[tc], in_=x_h[r0 + tc * 128: r0 + (tc + 1) * 128, :])
                xsem.inc(d, 16)
            xarr = xsem.n
            wait(VEC, "vec", xsem, xarr)
            wait(VEC, "vec", dv, attn_dv)
            for tc in range(2):
                v = VEC.tensor_tensor(xld[tc], xld[tc], srt(tc), ALU.add)
                dv.inc(v)
                state["sr_free"][tc] = (dv, dv.n)
            comb_dv = dv.n
            wait(GPS, "gps", dv, comb_dv)
            for tc in range(2):
                d = GPS.dma_start(out=out_h[r0 + tc * 128: r0 + (tc + 1) * 128, :],
                                  in_=xld[tc])
                xsem.inc(d, 16)

        GPS.wait_ge(xsem.h, xsem.n)
        GPS.wait_ge(msem.h, msem.n)
        GPS.wait_ge(aux.h, aux.n)
    return nc


_NC_CACHE = {}


def _get_nc():
    if "nc" not in _NC_CACHE:
        _NC_CACHE["nc"] = build_nc()
    return _NC_CACHE["nc"]


def kernel(L0_attn_data, L0_attn_weight, L0_attn_mask, L0_ffn_gate_w,
           L0_ffn_up_w, L0_ffn_down_w, L0_ffn_fc1_w, L0_ffn_fc2_w,
           L0_ffn_expert_w, L0_ffn_ln_gamma, L0_ffn_ln_beta):
    x = np.ascontiguousarray(np.asarray(L0_attn_data, dtype=np.float32).reshape(SEQ, D))
    wa = _relayout(np.ascontiguousarray(np.asarray(L0_attn_weight, dtype=np.float32).T))
    ma = _relayout(np.ascontiguousarray(np.asarray(L0_attn_mask, dtype=np.int32).T))
    wu = _relayout(np.asarray(L0_ffn_up_w, dtype=np.float32))
    wd = _relayout(np.asarray(L0_ffn_down_w, dtype=np.float32))
    w1 = _relayout(np.asarray(L0_ffn_fc1_w, dtype=np.float32))
    we = _relayout(np.asarray(L0_ffn_expert_w, dtype=np.float32))
    w2 = _relayout(np.asarray(L0_ffn_fc2_w, dtype=np.float32))
    gate = np.ascontiguousarray(
        np.asarray(L0_ffn_gate_w, dtype=np.float32).reshape(KC_D, 128, NE)
        .transpose(1, 0, 2).reshape(128, KC_D * NE))
    gb = np.ascontiguousarray(
        np.concatenate([np.asarray(L0_ffn_ln_gamma, dtype=np.float32),
                        np.asarray(L0_ffn_ln_beta, dtype=np.float32)]).reshape(1, 2 * D))

    nc = _get_nc()
    in_maps = []
    for c in range(NCORES):
        in_maps.append({
            "x": np.ascontiguousarray(x[c * TPC:(c + 1) * TPC, :]),
            "wa": wa, "ma": ma, "wu": wu, "wd": wd, "w1": w1, "we": we,
            "w2": w2, "gate": gate, "gb": gb,
        })
    res = run_bass_kernel_spmd(nc, in_maps, core_ids=list(range(NCORES)))
    out = np.concatenate([r["out"] for r in res.results], axis=0)
    return out.reshape(1, SEQ, D)


# version 07
# speedup vs baseline: 1.4265x; 1.4265x over previous
import math
import numpy as np
import concourse.bass as bass
import concourse.mybir as mybir
from concourse.bass_utils import run_bass_kernel_spmd

F32 = mybir.dt.float32
BF16 = mybir.dt.bfloat16
I32 = mybir.dt.int32
AF = mybir.ActivationFunctionType
ALU = mybir.AluOpType
AX = mybir.AxisListType

D = 4096
INT = 8192
LAT = 2048
NE = 8
SEQ = 8192
NCORES = 8
TPC = SEQ // NCORES
TILE = 256
NT = TPC // TILE
NS = 512
EPS = 1e-5
RING = 8
KC_D = D // 128
KC_I = INT // 128
KC_L = LAT // 128


def _relayout(w):
    K, N = w.shape
    return np.ascontiguousarray(
        w.reshape(K // 128, 128, N // NS, NS).transpose(2, 0, 1, 3).reshape(-1, NS))


class Sem:
    def __init__(self, h):
        self.h = h
        self.n = 0

    def inc(self, instr, k=1):
        instr.then_inc(self.h, k)
        self.n += k
        return instr


_PH = ["ln1", "gate", "up", "down", "fc1", "exp", "fc2", "gbc", "ln2",
       "attn", "comb"]


def build_nc(upto="comb", ntiles=NT):
    lim = _PH.index(upto)

    def stage(name):
        return _PH.index(name) <= lim

    nc = bass.Bass("TRN2", target_bir_lowering=False, debug=False)

    x_h = nc.dram_tensor("x", [TPC, D], F32, kind="ExternalInput").ap()
    out_h = nc.dram_tensor("out", [TPC, D], F32, kind="ExternalOutput").ap()
    gate_h = nc.dram_tensor("gate", [128, KC_D * NE], F32, kind="ExternalInput").ap()
    gb_h = nc.dram_tensor("gb", [1, 2 * D], F32, kind="ExternalInput").ap()
    wa_h = nc.dram_tensor("wa", [(D // NS) * KC_D * 128, NS], F32, kind="ExternalInput").ap()
    ma_h = nc.dram_tensor("ma", [(D // NS) * KC_D * 128, NS], I32, kind="ExternalInput").ap()
    wu_h = nc.dram_tensor("wu", [(INT // NS) * KC_D * 128, NS], F32, kind="ExternalInput").ap()
    wd_h = nc.dram_tensor("wd", [(D // NS) * KC_I * 128, NS], F32, kind="ExternalInput").ap()
    w1_h = nc.dram_tensor("w1", [(LAT // NS) * KC_D * 128, NS], F32, kind="ExternalInput").ap()
    we_h = nc.dram_tensor("we", [(LAT // NS) * KC_L * 128, NS], F32, kind="ExternalInput").ap()
    w2_h = nc.dram_tensor("w2", [(D // NS) * KC_L * 128, NS], F32, kind="ExternalInput").ap()

    S = lambda n_, s, dt=F32: nc.alloc_sbuf_tensor(n_, s, dt).ap()
    xn = S("xn", [128, KC_D * 256], BF16)
    xnl = S("xnl", [128, KC_D * 256], BF16)
    hb = S("hb", [128, KC_I * 256])
    u32 = S("u32", [128, 2 * KC_L * 256])
    sr = S("sr", [128, 2 * D])
    stg = S("stg", [128, 4 * NS])
    wring = S("wring", [128, RING * NS])
    whi = S("whi", [128, RING * NS], BF16)
    wlo = S("wlo", [128, RING * NS], BF16)
    gate_s = S("gate_s", [128, KC_D * NE])
    gate_hi = S("gate_hi", [128, KC_D * NE], BF16)
    gate_lo = S("gate_lo", [128, KC_D * NE], BF16)
    ones1 = S("ones1", [1, 128])
    ident = S("ident", [128, 128])
    iot = S("iot", [128, 128], I32)
    logit = S("logit", [128, 2 * NE])
    lm = S("lm", [128, 2 * NE])
    sm = {k: S(k, [128, 2]) for k in
          ["route", "m1", "cnt", "m2", "tta", "ttb", "red", "ssq",
           "mu", "msq", "mu2", "veps", "stdv", "rstd"]}

    P = lambda n_: nc.alloc_psum_tensor(n_, [128, NS], F32).ap()
    psA = [P("psA0"), P("psA1")]
    psB = [P("psB0"), P("psB1")]
    pT = [P("pT0"), P("pT1")]
    psG = P("psG")
    psG2 = P("psG2")

    xld = [hb[:, 0:D], hb[:, D:2 * D]]
    xnt = [hb[:, 2 * D:3 * D], hb[:, 3 * D:4 * D]]
    gbrow = hb[0:1, 2 * D:4 * D]
    hbb = hb.bitcast(BF16)
    u32b = u32.bitcast(BF16)
    HL = KC_I * 256
    UL = KC_L * 256
    ck = lambda t, off, c, tc: t[:, off + c * 256 + tc * 128: off + c * 256 + tc * 128 + 128]
    latc = lambda lc, tc: ck(u32b, 0, lc, tc)
    latlc = lambda lc, tc: ck(u32b, UL, lc, tc)
    moec = lambda lc, tc: ck(u32b, 2 * UL, lc, tc)
    moelc = lambda lc, tc: ck(u32b, 3 * UL, lc, tc)
    gbc = u32[:, 0:D]
    bbc = u32[:, D:2 * D]
    xnc = lambda kc, tc: ck(xn, 0, kc, tc)
    xnlc = lambda kc, tc: ck(xnl, 0, kc, tc)
    hc = lambda ic, tc: ck(hbb, 0, ic, tc)
    hlc = lambda ic, tc: ck(hbb, HL, ic, tc)
    srt = lambda tc: sr[:, tc * D:(tc + 1) * D]

    with (
        nc.semaphore("wsA") as _wsA,
        nc.semaphore("wsB") as _wsB,
        nc.semaphore("msem") as _ms,
        nc.semaphore("xsem") as _xs,
        nc.semaphore("aux") as _ax,
        nc.semaphore("ucons") as _uc,
        nc.semaphore("tmm") as _tmm,
        nc.semaphore("ev") as _ev,
        nc.semaphore("dv") as _dv,
        nc.Block() as block,
    ):
        wsem = [Sem(_wsA), Sem(_wsB)]
        msem, xsem, aux = Sem(_ms), Sem(_xs), Sem(_ax)
        ucons, tmm, ev, dv = Sem(_uc), Sem(_tmm), Sem(_ev), Sem(_dv)
        last_wait = {}

        def wait(eng, ename, sem, val):
            if val <= 0:
                return
            key = (ename, sem.h.name if hasattr(sem.h, "name") else id(sem.h))
            if last_wait.get(key, -1) >= val:
                return
            last_wait[key] = val
            eng.wait_ge(sem.h, val)

        TEN, VEC, ACT, SYN, GPS = nc.tensor, nc.vector, nc.scalar, nc.sync, nc.gpsimd

        gU = [0]
        bank_free = {}
        stg_free = {}
        pT_free = {}
        state = {"sr_free": [(dv, 0), (dv, 0)], "u32_free": (dv, 0),
                 "logit_free": (dv, 0), "stg_mask_free": (dv, 0),
                 "route_free": (dv, 0), "lo_dv": [], "mask_dv": 0}

        def wslot_ap(u):
            s = u % RING
            return wring[:, s * NS:(s + 1) * NS]

        def issue_units(w_hbm, base_unit, count):
            assert count == 4
            u0 = gU[0]
            b = u0 // 4
            half = wsem[b % 2]
            for j in range(count):
                r = (base_unit + j) * 128
                d = SYN.dma_start(out=wslot_ap(u0 + j), in_=w_hbm[r:r + 128, :])
                half.inc(d, 16)
            gU[0] += count
            return half, half.n

        def mask_issue4(m_hbm, base_unit):
            for j in range(4):
                r = (base_unit + j) * 128
                d = GPS.dma_start(out=stg[:, j * NS:(j + 1) * NS], in_=m_hbm[r:r + 128, :])
                msem.inc(d, 16)
            return msem.n

        def accum_phase(tag, nslabs, kcs, par0, lhsT_hi, lhsT_lo, w_hbm,
                        w_base_unit, evac_fn, m_hbm=None):
            pending = None
            for s_ in range(nslabs):
                par = (par0 + s_) % 2
                banks = psA if par == 0 else psB
                bname = ("psA", "psB")[par]
                for b, pb in zip(banks, (bname + "0", bname + "1")):
                    if pb in bank_free:
                        sem, val = bank_free[pb]
                        wait(TEN, "ten", sem, val)
                for kb in range(kcs // 4):
                    base = s_ * kcs + kb * 4
                    bglob = gU[0] // 4
                    k2 = bglob - RING // 4
                    if k2 >= 0 and k2 < len(state["lo_dv"]):
                        wait(SYN, "syn", dv, state["lo_dv"][k2])
                    wsm, arr = issue_units(w_hbm, w_base_unit + base, 4)
                    u0 = gU[0] - 4
                    s0 = u0 % RING
                    span = slice(s0 * NS, (s0 + 4) * NS)
                    hname = "AB"[wsm is wsem[1]]
                    if m_hbm is not None:
                        wait(GPS, "gps", dv, state["mask_dv"])
                        marr = mask_issue4(m_hbm, w_base_unit + base)
                        wait(VEC, "vec" + hname, wsm, arr)
                        wait(VEC, "vec", msem, marr)
                        t = VEC.tensor_tensor(wring[:, span], wring[:, span],
                                              stg[:, 0:4 * NS], ALU.mult)
                        dv.inc(t)
                        state["mask_dv"] = dv.n
                        wait(ACT, "act", dv, dv.n)
                    else:
                        wait(ACT, "act" + hname, wsm, arr)
                    wait(ACT, "act", ucons, bglob - 1)
                    a_ = ACT.copy(whi[:, span], wring[:, span])
                    ev.inc(a_)
                    wait(VEC, "vec", ev, ev.n)
                    t = VEC.tensor_tensor(wlo[:, span], wring[:, span],
                                          whi[:, span], ALU.subtract)
                    dv.inc(t)
                    state["lo_dv"].append(dv.n)
                    wait(TEN, "ten", dv, dv.n)
                    for j in range(4):
                        kc = kb * 4 + j
                        u = u0 + j
                        sl_hi = whi[:, (u % RING) * NS:(u % RING + 1) * NS]
                        sl_lo = wlo[:, (u % RING) * NS:(u % RING + 1) * NS]
                        st_ = kc == 0
                        sp_ = kc == kcs - 1
                        for tc in range(2):
                            TEN.matmul(banks[tc], lhsT_hi(kc, tc), sl_hi,
                                       start=st_, stop=False)
                            TEN.matmul(banks[tc], lhsT_hi(kc, tc), sl_lo,
                                       start=False, stop=False)
                            m1_ = TEN.matmul(banks[tc], lhsT_lo(kc, tc), sl_hi,
                                             start=False, stop=sp_)
                        if j == 3:
                            ucons.inc(m1_)
                    if kb == 0 and pending is not None:
                        evac_fn(*pending)
                        pending = None
                pending = (s_, banks, (bname + "0", bname + "1"), ucons.n)
            evac_fn(*pending)

        def stage_to_fm(tc, src_slab, col_hi, col_lo, dep_sem, dep_val):
            if tc in pT_free:
                sem, val = pT_free[tc]
                wait(TEN, "ten", sem, val)
            wait(TEN, "ten", dep_sem, dep_val)
            for j in range(4):
                t = TEN.transpose(pT[tc][:, j * 128:(j + 1) * 128],
                                  src_slab[:, j * 128:(j + 1) * 128], ident)
            tmm.inc(t)
            wait(ACT, "act", tmm, tmm.n)
            for j in range(4):
                c = ACT.copy(col_hi(j), pT[tc][:, j * 128:(j + 1) * 128])
            ev.inc(c)
            wait(VEC, "vec", ev, ev.n)
            for j in range(4):
                v = VEC.tensor_tensor(col_lo(j), pT[tc][:, j * 128:(j + 1) * 128],
                                      col_hi(j), ALU.subtract)
            dv.inc(v)
            pT_free[tc] = (dv, dv.n)

        GPS.iota(iot, [[1, 128]], base=0, channel_multiplier=-1)
        d = GPS.dma_start(out=gate_s, in_=gate_h)
        aux.inc(d, 16)
        aux_setup = aux.n
        VEC.memset(ones1, 1.0)
        wait(VEC, "vec", aux, aux_setup)
        v = VEC.tensor_scalar(ident, iot, 0.0, None, ALU.is_equal)
        dv.inc(v)
        wait(ACT, "act", aux, aux_setup)
        a = ACT.copy(gate_hi, gate_s)
        ev.inc(a)
        wait(VEC, "vec", ev, ev.n)
        v = VEC.tensor_tensor(gate_lo, gate_s, gate_hi, ALU.subtract)
        dv.inc(v)
        setup_dv = dv.n

        for t in range(ntiles):
            r0 = t * TILE

            wait(GPS, "gps", ucons, ucons.n)
            wait(GPS, "gps", tmm, tmm.n)
            wait(GPS, "gps", ev, ev.n)
            wait(GPS, "gps", xsem, xsem.n)
            for tc in range(2):
                d = GPS.dma_start(out=xld[tc], in_=x_h[r0 + tc * 128: r0 + (tc + 1) * 128, :])
                xsem.inc(d, 16)
            xarr = xsem.n
            wait(VEC, "vec", xsem, xarr)
            wait(ACT, "act", xsem, xarr)
            for tc in range(2):
                v = VEC.reduce_sum(sm["red"][:, tc:tc + 1], xld[tc], axis=AX.X)
            dv.inc(v)
            wait(VEC, "vec", dv, dv.n)
            for tc in range(2):
                a = ACT.activation(xnt[tc], xld[tc], AF.Square,
                                   accum_out=sm["ssq"][:, tc:tc + 1])
            ev.inc(a)
            for tc in range(2):
                v = VEC.tensor_scalar(sm["mu"][:, tc:tc + 1], sm["red"][:, tc:tc + 1],
                                      1.0 / D, None, ALU.mult)
            dv.inc(v)
            wait(VEC, "vec", dv, dv.n)
            wait(VEC, "vec", ev, ev.n)
            for tc in range(2):
                VEC.tensor_scalar(sm["msq"][:, tc:tc + 1], sm["ssq"][:, tc:tc + 1],
                                  1.0 / D, EPS, ALU.mult, ALU.add)
            for tc in range(2):
                v = VEC.tensor_tensor(sm["mu2"][:, tc:tc + 1], sm["mu"][:, tc:tc + 1],
                                      sm["mu"][:, tc:tc + 1], ALU.mult)
            dv.inc(v)
            wait(VEC, "vec", dv, dv.n)
            for tc in range(2):
                v = VEC.tensor_tensor(sm["veps"][:, tc:tc + 1], sm["msq"][:, tc:tc + 1],
                                      sm["mu2"][:, tc:tc + 1], ALU.subtract)
            dv.inc(v)
            wait(ACT, "act", dv, dv.n)
            for tc in range(2):
                a = ACT.activation(sm["stdv"][:, tc:tc + 1], sm["veps"][:, tc:tc + 1], AF.Sqrt)
            ev.inc(a)
            wait(VEC, "vec", ev, ev.n)
            for tc in range(2):
                v = VEC.reciprocal(sm["rstd"][:, tc:tc + 1], sm["stdv"][:, tc:tc + 1])
            dv.inc(v)
            wait(VEC, "vec", dv, dv.n)
            apply_dv = []
            for tc in range(2):
                v = VEC.tensor_scalar(xnt[tc], xld[tc], sm["mu"][:, tc:tc + 1],
                                      sm["rstd"][:, tc:tc + 1], ALU.subtract, ALU.mult)
                dv.inc(v)
                apply_dv.append(dv.n)
            for tc in range(2):
                for q in range(8):
                    stage_to_fm(tc, xnt[tc][:, q * NS:(q + 1) * NS],
                                lambda j, q=q, tc=tc: xnc(q * 4 + j, tc),
                                lambda j, q=q, tc=tc: xnlc(q * 4 + j, tc),
                                dv, apply_dv[tc])
            xn_ev = ev.n
            xn_dv = dv.n

            if not stage("gate"):
                continue
            if "psG" in bank_free:
                sem, val = bank_free["psG"]
                wait(TEN, "ten", sem, val)
            wait(TEN, "ten", ev, xn_ev)
            wait(TEN, "ten", aux, aux_setup)
            if "psG2" in bank_free:
                sem, val = bank_free["psG2"]
                wait(TEN, "ten", sem, val)
            wait(TEN, "ten", dv, max(setup_dv, xn_dv))
            for kc in range(KC_D):
                st_ = kc == 0
                sp_ = kc == KC_D - 1
                ghi = gate_hi[:, kc * NE:(kc + 1) * NE]
                glo = gate_lo[:, kc * NE:(kc + 1) * NE]
                for tc, pg in ((0, psG), (1, psG2)):
                    TEN.matmul(pg[:, 0:NE], xnc(kc, tc), ghi, start=st_, stop=False)
                    TEN.matmul(pg[:, 0:NE], xnc(kc, tc), glo, start=False, stop=False)
                    m_ = TEN.matmul(pg[:, 0:NE], xnlc(kc, tc), ghi, start=False, stop=sp_)
            tmm.inc(m_)
            sem, val = state["logit_free"]
            wait(ACT, "act", sem, val)
            wait(ACT, "act", tmm, tmm.n)
            a = ACT.copy(logit[:, 0:NE], psG[:, 0:NE])
            a = ACT.copy(logit[:, NE:2 * NE], psG2[:, 0:NE])
            ev.inc(a)
            bank_free["psG"] = (ev, ev.n)
            bank_free["psG2"] = (ev, ev.n)
            wait(VEC, "vec", ev, ev.n)
            sem, val = state["route_free"]
            wait(VEC, "vec", sem, val)
            l8 = lambda tc: logit[:, tc * NE:(tc + 1) * NE]
            c1 = lambda nm, tc: sm[nm][:, tc:tc + 1]

            def vfence(instr):
                dv.inc(instr)
                wait(VEC, "vec", dv, dv.n)

            for tc in range(2):
                v = VEC.reduce_max(c1("m1", tc), l8(tc), axis=AX.X)
            vfence(v)
            for tc in range(2):
                v = VEC.tensor_scalar(lm[:, tc * NE:(tc + 1) * NE], l8(tc),
                                      c1("m1", tc), None, ALU.is_ge)
            vfence(v)
            for tc in range(2):
                v = VEC.reduce_sum(c1("cnt", tc), lm[:, tc * NE:(tc + 1) * NE], axis=AX.X)
            vfence(v)
            for tc in range(2):
                v = VEC.scalar_tensor_tensor(lm[:, tc * NE:(tc + 1) * NE],
                                             lm[:, tc * NE:(tc + 1) * NE], -1e30,
                                             l8(tc), ALU.mult, ALU.add)
            vfence(v)
            for tc in range(2):
                v = VEC.reduce_max(c1("m2", tc), lm[:, tc * NE:(tc + 1) * NE], axis=AX.X)
            vfence(v)
            for tc in range(2):
                v = VEC.tensor_scalar(c1("cnt", tc), c1("cnt", tc), 1.5, None, ALU.is_ge)
            vfence(v)
            for tc in range(2):
                v = VEC.tensor_tensor(c1("tta", tc), c1("m1", tc), c1("m2", tc), ALU.subtract)
            for tc in range(2):
                v2 = VEC.tensor_tensor(c1("ttb", tc), c1("m1", tc), c1("m2", tc), ALU.add)
            vfence(v2)
            for tc in range(2):
                v = VEC.tensor_tensor(c1("tta", tc), c1("tta", tc), c1("cnt", tc), ALU.mult)
            vfence(v)
            for tc in range(2):
                v = VEC.tensor_tensor(c1("route", tc), c1("ttb", tc), c1("tta", tc), ALU.add)
            dv.inc(v)
            state["logit_free"] = (dv, dv.n)
            route_dv = dv.n
            if upto == "gate":
                wait(GPS, "gps", dv, dv.n)
                for nm_i, nm in enumerate(["m1", "m2", "cnt", "route"]):
                    d = GPS.dma_start(out=out_h[nm_i * 128:(nm_i + 1) * 128, 0:2],
                                      in_=sm[nm])
                    xsem.inc(d, 16)
                d = GPS.dma_start(out=out_h[512:640, 0:2 * NE], in_=logit)
                xsem.inc(d, 16)

            if not stage("up"):
                continue
            sem, val = state["stg_mask_free"]

            def evac_up(s_, banks, bn, tval):
                for tc in range(2):
                    slab = stg[:, ((s_ % 2) * 2 + tc) * NS:((s_ % 2) * 2 + tc + 1) * NS]
                    key = (s_ % 2) * 2 + tc
                    if key in stg_free:
                        sm_, vl_ = stg_free[key]
                        wait(ACT, "act", sm_, vl_)
                    wait(ACT, "act", ucons, tval)
                    a = ACT.activation(slab, banks[tc], AF.Relu)
                    ev.inc(a)
                    bank_free[bn[tc]] = (ev, ev.n)
                    wait(VEC, "vec", ev, ev.n)
                    v = VEC.tensor_tensor(slab, slab, slab, ALU.mult)
                    dv.inc(v)
                    stage_to_fm(tc, slab,
                                lambda j, s_=s_, tc=tc: hc(s_ * 4 + j, tc),
                                lambda j, s_=s_, tc=tc: hlc(s_ * 4 + j, tc),
                                dv, dv.n)
                    stg_free[key] = (tmm, tmm.n)

            wait(ACT, "act", sem, val)
            wait(TEN, "ten", ev, xn_ev)
            wait(TEN, "ten", dv, xn_dv)
            accum_phase("up", INT // NS, KC_D, 0, xnc, xnlc, wu_h, 0, evac_up)
            h_ev = ev.n
            h_dv = dv.n

            if not stage("down"):
                continue
            def evac_down(s_, banks, bn, tval):
                wait(ACT, "act", ucons, tval)
                for tc in range(2):
                    sm_, vl_ = state["sr_free"][tc]
                    wait(ACT, "act", sm_, vl_)
                    a = ACT.copy(srt(tc)[:, s_ * NS:(s_ + 1) * NS], banks[tc])
                    ev.inc(a)
                    bank_free[bn[tc]] = (ev, ev.n)

            wait(TEN, "ten", ev, h_ev)
            wait(TEN, "ten", dv, h_dv)
            accum_phase("down", D // NS, KC_I, 0, hc, hlc, wd_h, 0, evac_down)

            if not stage("fc1"):
                continue
            sem, val = state["u32_free"]

            def evac_fc1(s_, banks, bn, tval):
                for tc in range(2):
                    slab = stg[:, ((s_ % 2) * 2 + tc) * NS:((s_ % 2) * 2 + tc + 1) * NS]
                    key = (s_ % 2) * 2 + tc
                    if key in stg_free:
                        sm_, vl_ = stg_free[key]
                        wait(ACT, "act", sm_, vl_)
                    wait(ACT, "act", ucons, tval)
                    sm2_, vl2_ = state["u32_free"]
                    wait(ACT, "act", sm2_, vl2_)
                    a = ACT.copy(slab, banks[tc])
                    ev.inc(a)
                    bank_free[bn[tc]] = (ev, ev.n)
                    stage_to_fm(tc, slab,
                                lambda j, s_=s_, tc=tc: latc(s_ * 4 + j, tc),
                                lambda j, s_=s_, tc=tc: latlc(s_ * 4 + j, tc),
                                ev, ev.n)
                    stg_free[key] = (tmm, tmm.n)

            accum_phase("fc1", LAT // NS, KC_D, 0, xnc, xnlc, w1_h, 0, evac_fc1)
            lat_ev = ev.n
            lat_dv = dv.n

            if not stage("exp"):
                continue
            def evac_exp(s_, banks, bn, tval):
                for tc in range(2):
                    slab = stg[:, ((s_ % 2) * 2 + tc) * NS:((s_ % 2) * 2 + tc + 1) * NS]
                    key = (s_ % 2) * 2 + tc
                    if key in stg_free:
                        sm_, vl_ = stg_free[key]
                        wait(VEC, "vec", sm_, vl_)
                    wait(VEC, "vec", ucons, tval)
                    v = VEC.tensor_scalar(slab, banks[tc], sm["route"][:, tc:tc + 1],
                                          None, ALU.mult)
                    dv.inc(v)
                    bank_free[bn[tc]] = (dv, dv.n)
                    stage_to_fm(tc, slab,
                                lambda j, s_=s_, tc=tc: moec(s_ * 4 + j, tc),
                                lambda j, s_=s_, tc=tc: moelc(s_ * 4 + j, tc),
                                dv, dv.n)
                    stg_free[key] = (tmm, tmm.n)

            wait(TEN, "ten", ev, lat_ev)
            wait(TEN, "ten", dv, lat_dv)
            accum_phase("exp", LAT // NS, KC_L, 0, latc, latlc, we_h, 0, evac_exp)
            moe_ev = ev.n
            moe_dv = dv.n

            if not stage("fc2"):
                continue
            def evac_fc2(s_, banks, bn, tval):
                wait(VEC, "vec", ucons, tval)
                for tc in range(2):
                    dst = srt(tc)[:, s_ * NS:(s_ + 1) * NS]
                    v = VEC.tensor_tensor(dst, dst, banks[tc], ALU.add)
                    dv.inc(v)
                    bank_free[bn[tc]] = (dv, dv.n)

            wait(TEN, "ten", ev, moe_ev)
            wait(TEN, "ten", dv, moe_dv)
            accum_phase("fc2", D // NS, KC_L, 0, moec, moelc, w2_h, 0, evac_fc2)
            fc2_dv = dv.n

            if not stage("gbc"):
                continue
            wait(GPS, "gps", tmm, tmm.n)
            wait(GPS, "gps", ucons, ucons.n)
            d = GPS.dma_start(out=gbrow, in_=gb_h)
            aux.inc(d, 16)
            wait(TEN, "ten", aux, aux.n)
            pingpong = [psG, pT[0], pT[1]]
            ppn = ["psG", "pTg0", "pTg1"]
            for i in range(2 * D // NS):
                pp = pingpong[i % 3]
                nm = ppn[i % 3]
                if nm in bank_free:
                    sem, val = bank_free[nm]
                    wait(TEN, "ten", sem, val)
                if nm.startswith("pTg") and (i % 3) - 1 in pT_free:
                    sem, val = pT_free[(i % 3) - 1]
                    wait(TEN, "ten", sem, val)
                m_ = TEN.matmul(pp, ones1, gbrow[0:1, i * NS:(i + 1) * NS],
                                start=True, stop=True)
                tmm.inc(m_)
                wait(ACT, "act", tmm, tmm.n)
                a = ACT.copy(u32[:, i * NS:(i + 1) * NS], pp)
                ev.inc(a)
                bank_free[nm] = (ev, ev.n)
                if nm.startswith("pTg"):
                    pT_free[(i % 3) - 1] = (ev, ev.n)
            gbc_ev = ev.n

            if not stage("ln2"):
                continue
            wait(VEC, "vec", dv, fc2_dv)
            for tc in range(2):
                v = VEC.reduce_sum(sm["red"][:, tc:tc + 1], srt(tc), axis=AX.X)
            dv.inc(v)
            wait(VEC, "vec", dv, dv.n)
            wait(ACT, "act", dv, fc2_dv)
            for tc in range(2):
                a = ACT.activation(xld[tc], srt(tc), AF.Square,
                                   accum_out=sm["ssq"][:, tc:tc + 1])
            ev.inc(a)
            for tc in range(2):
                v = VEC.tensor_scalar(sm["mu"][:, tc:tc + 1], sm["red"][:, tc:tc + 1],
                                      1.0 / D, None, ALU.mult)
            dv.inc(v)
            wait(VEC, "vec", dv, dv.n)
            wait(VEC, "vec", ev, ev.n)
            for tc in range(2):
                VEC.tensor_scalar(sm["msq"][:, tc:tc + 1], sm["ssq"][:, tc:tc + 1],
                                  1.0 / D, EPS, ALU.mult, ALU.add)
            for tc in range(2):
                v = VEC.tensor_tensor(sm["mu2"][:, tc:tc + 1], sm["mu"][:, tc:tc + 1],
                                      sm["mu"][:, tc:tc + 1], ALU.mult)
            dv.inc(v)
            wait(VEC, "vec", dv, dv.n)
            for tc in range(2):
                v = VEC.tensor_tensor(sm["veps"][:, tc:tc + 1], sm["msq"][:, tc:tc + 1],
                                      sm["mu2"][:, tc:tc + 1], ALU.subtract)
            dv.inc(v)
            wait(ACT, "act", dv, dv.n)
            for tc in range(2):
                a = ACT.activation(sm["stdv"][:, tc:tc + 1], sm["veps"][:, tc:tc + 1], AF.Sqrt)
            ev.inc(a)
            ln2sq_ev = ev.n
            wait(VEC, "vec", ev, ev.n)
            for tc in range(2):
                v = VEC.reciprocal(sm["rstd"][:, tc:tc + 1], sm["stdv"][:, tc:tc + 1])
            dv.inc(v)
            wait(VEC, "vec", dv, dv.n)
            wait(VEC, "vec", ev, gbc_ev)
            for tc in range(2):
                v = VEC.tensor_scalar(srt(tc), srt(tc), sm["mu"][:, tc:tc + 1],
                                      sm["rstd"][:, tc:tc + 1], ALU.subtract, ALU.mult)
            dv.inc(v)
            wait(VEC, "vec", dv, dv.n)
            for tc in range(2):
                v = VEC.tensor_tensor(srt(tc), srt(tc), gbc, ALU.mult)
            dv.inc(v)
            wait(VEC, "vec", dv, dv.n)
            for tc in range(2):
                v = VEC.tensor_tensor(srt(tc), srt(tc), bbc, ALU.add)
            dv.inc(v)
            state["u32_free"] = (dv, dv.n)
            state["route_free"] = (dv, dv.n)

            if not stage("attn"):
                continue
            def evac_attn(s_, banks, bn, tval):
                wait(VEC, "vec", ucons, tval)
                for tc in range(2):
                    dst = srt(tc)[:, s_ * NS:(s_ + 1) * NS]
                    v = VEC.tensor_tensor(dst, dst, banks[tc], ALU.add)
                    dv.inc(v)
                    bank_free[bn[tc]] = (dv, dv.n)

            for key in range(4):
                if key in stg_free:
                    sm_, vl_ = stg_free[key]
                    wait(GPS, "gps", sm_, vl_)
            accum_phase("attn", D // NS, KC_D, 0, xnc, xnlc, wa_h, 0, evac_attn,
                        m_hbm=ma_h)
            state["stg_mask_free"] = (dv, dv.n)
            attn_dv = dv.n
            state["sr_free"] = [(dv, attn_dv), (dv, attn_dv)]

            if not stage("comb"):
                continue
            wait(GPS, "gps", ev, ln2sq_ev)
            for tc in range(2):
                d = GPS.dma_start(out=xld[tc], in_=x_h[r0 + tc * 128: r0 + (tc + 1) * 128, :])
                xsem.inc(d, 16)
            xarr = xsem.n
            wait(VEC, "vec", xsem, xarr)
            wait(VEC, "vec", dv, attn_dv)
            for tc in range(2):
                v = VEC.tensor_tensor(xld[tc], xld[tc], srt(tc), ALU.add)
                dv.inc(v)
                state["sr_free"][tc] = (dv, dv.n)
            comb_dv = dv.n
            wait(GPS, "gps", dv, comb_dv)
            for tc in range(2):
                d = GPS.dma_start(out=out_h[r0 + tc * 128: r0 + (tc + 1) * 128, :],
                                  in_=xld[tc])
                xsem.inc(d, 16)

        GPS.wait_ge(xsem.h, xsem.n)
        GPS.wait_ge(msem.h, msem.n)
        GPS.wait_ge(aux.h, aux.n)
    return nc


_NC_CACHE = {}


def _get_nc():
    if "nc" not in _NC_CACHE:
        _NC_CACHE["nc"] = build_nc()
    return _NC_CACHE["nc"]


def kernel(L0_attn_data, L0_attn_weight, L0_attn_mask, L0_ffn_gate_w,
           L0_ffn_up_w, L0_ffn_down_w, L0_ffn_fc1_w, L0_ffn_fc2_w,
           L0_ffn_expert_w, L0_ffn_ln_gamma, L0_ffn_ln_beta):
    x = np.ascontiguousarray(np.asarray(L0_attn_data, dtype=np.float32).reshape(SEQ, D))
    wa = _relayout(np.ascontiguousarray(np.asarray(L0_attn_weight, dtype=np.float32).T))
    ma = _relayout(np.ascontiguousarray(np.asarray(L0_attn_mask, dtype=np.int32).T))
    wu = _relayout(np.asarray(L0_ffn_up_w, dtype=np.float32))
    wd = _relayout(np.asarray(L0_ffn_down_w, dtype=np.float32))
    w1 = _relayout(np.asarray(L0_ffn_fc1_w, dtype=np.float32))
    we = _relayout(np.asarray(L0_ffn_expert_w, dtype=np.float32))
    w2 = _relayout(np.asarray(L0_ffn_fc2_w, dtype=np.float32))
    gate = np.ascontiguousarray(
        np.asarray(L0_ffn_gate_w, dtype=np.float32).reshape(KC_D, 128, NE)
        .transpose(1, 0, 2).reshape(128, KC_D * NE))
    gb = np.ascontiguousarray(
        np.concatenate([np.asarray(L0_ffn_ln_gamma, dtype=np.float32),
                        np.asarray(L0_ffn_ln_beta, dtype=np.float32)]).reshape(1, 2 * D))

    nc = _get_nc()
    in_maps = []
    for c in range(NCORES):
        in_maps.append({
            "x": np.ascontiguousarray(x[c * TPC:(c + 1) * TPC, :]),
            "wa": wa, "ma": ma, "wu": wu, "wd": wd, "w1": w1, "we": we,
            "w2": w2, "gate": gate, "gb": gb,
        })
    res = run_bass_kernel_spmd(nc, in_maps, core_ids=list(range(NCORES)))
    out = np.concatenate([r["out"] for r in res.results], axis=0)
    return out.reshape(1, SEQ, D)
